# version 12 (final)
# baseline (speedup 1.0000x reference)
import numpy as np
import ml_dtypes
import concourse.bass as bass
import concourse.mybir as mybir
from concourse.bass_utils import run_bass_kernel_spmd
from concourse.bass_isa import ReduceOp
from concourse import library_config

F32 = mybir.dt.float32
BF16 = mybir.dt.bfloat16
AF = mybir.ActivationFunctionType
ALU = mybir.AluOpType
AX = mybir.AxisListType

NCORE = 8
TC = 1024
NT = 8
D = 2048
H = 16
HID = 8192
NDC = D // 128
NHC = HID // 128
MAGIC = float(np.float32(1.5 * 2 ** 23))
ATANH_HALF = 0.5493061443340549
NW_ELEM = float(HID * D)

_DBG = []
_STAGE = "full"


_RACECHECK = False


def _build(stage, dbg):
    nc = bass.Bass("TRN2", target_bir_lowering=False,
                   detect_race_conditions=_RACECHECK)
    dt = nc.dram_tensor

    x_in = dt("x", [TC, D], F32, kind="ExternalInput").ap()
    xT_in = dt("xT", [D, TC], F32, kind="ExternalInput").ap()
    wqkv_in = dt("w_qkv", [D, 3 * D], F32, kind="ExternalInput").ap()
    wout_in = dt("w_out", [D, D], F32, kind="ExternalInput").ap()
    w1T_in = dt("w1T", [D, HID], F32, kind="ExternalInput").ap()
    w2T_in = dt("w2T", [D, HID], F32, kind="ExternalInput").ap()
    w3T_in = dt("w3T", [HID, D], F32, kind="ExternalInput").ap()
    ws1_in = dt("ws1", [256, HID], F32, kind="ExternalInput").ap()
    ws2_in = dt("ws2", [256, HID], F32, kind="ExternalInput").ap()
    ws3_in = dt("ws3", [1024, D], F32, kind="ExternalInput").ap()
    g1r_in = dt("g1r", [128, D], F32, kind="ExternalInput").ap()
    g2r_in = dt("g2r", [128, D], F32, kind="ExternalInput").ap()
    g3r_in = dt("g3r", [128, HID], F32, kind="ExternalInput").ap()
    drep_in = dt("drep", [128, H], F32, kind="ExternalInput").ap()
    d16_in = dt("d16", [16, 1], F32, kind="ExternalInput").ap()
    mmat_in = dt("mmat", [128, 128], F32, kind="ExternalInput").ap()
    rows127_in = dt("rows127", [128, H], F32, kind="ExternalInput").ap()
    iota16_in = dt("iota16", [16, 128], F32, kind="ExternalInput").ap()
    cexpo_in = dt("cexpo", [16, 8], F32, kind="ExternalInput").ap()
    eexpo_in = dt("eexpo", [16, 8], F32, kind="ExternalInput").ap()
    ident_in = dt("identb", [128, 128], BF16, kind="ExternalInput").ap()
    eye16_in = dt("eye16", [16, 16], F32, kind="ExternalInput").ap()
    onesf_in = dt("onesf", [128, 128], F32, kind="ExternalInput").ap()

    out_t = dt("out", [TC, D], F32, kind="ExternalOutput").ap()

    cc_in = dt("cc_in", [128, 260], F32)
    cc_out = dt("cc_out", [8, 128, 260], F32)
    pp_d = dt("pp_d", [2048], F32)
    strow_d = dt("strow_d", [8, 4096], F32)
    spl_x1 = dt("spl_x1", [NT, 128, D], BF16)
    spl_q2T = dt("spl_q2T", [NT, 128, D], BF16)
    spl_z1 = dt("spl_z1", [NT, 128, HID], BF16)
    spl_h = dt("spl_h", [NT, 128, HID], BF16)
    spl_q3T = dt("spl_q3T", [NT, 128, HID], BF16)

    DBGSPEC = {
        "dbg_q": ([128, NT, D], BF16), "dbg_kvk": ([128, NT, 2 * D], BF16),
        "dbg_sums": ([16, NT, 256], F32), "dbg_stin": ([16, NT, 256], F32),
        "dbg_sw": ([128, 16], F32), "dbg_o": ([NT, 128, D], BF16),
        "dbg_r": ([NT, 128, D], BF16), "dbg_xq1": ([NT, 128, D], BF16),
        "dbg_rs": ([128, 3, NT], F32),
        "dbg_bst": ([128, NT, 8], F32), "dbg_qm": ([128, NT, 8], F32),
        "dbg_rcpB": ([128, NT, 4], F32), "dbg_z1": ([NT, 128, HID], BF16),
        "dbg_h": ([NT, 128, HID], BF16), "dbg_q3": ([NT, 128, HID], BF16),
        "dbg_est": ([128, NT, 8], F32), "dbg_f1": ([NT, 128, HID], F32),
    }
    dbg_t = {n: dt(n, DBGSPEC[n][0], DBGSPEC[n][1], kind="ExternalOutput").ap()
             for n in dbg}

    sem = {}
    cnt = {}

    def inc(ins, name, amt):
        ins.then_inc(sem[name], amt)
        cnt[name] += amt
        return cnt[name]

    def wt(eng, name, val):
        if val > 0:
            eng.wait_ge(sem[name], val)

    def inct(ins, name, amt=16):
        cnt[name] += amt
        ins.then_inc(sem[name], amt)
        return (name, cnt[name])

    def wtt(eng, token):
        eng.wait_ge(sem[token[0]], token[1])

    from contextlib import ExitStack
    with ExitStack() as st0:
        E = st0.enter_context
        for s in (["g", "t", "s", "v", "q", "cc", "pW", "pA", "pS", "pB", "pB2",
                   "pB3", "pB4", "pB5", "pB6", "pO", "pT1", "pT3", "pQ", "pR",
                   "pC", "pC2", "pD", "pE", "pE2", "pF", "pF2", "pQ3", "gc", "dq",
                   "gsu", "gxT", "gpc", "gcc", "gG", "gstw", "gsr", "gg12",
                   "gst0", "gst1", "gwq0", "gwq1", "gwo0", "gwo1", "gxs0", "gxs1",
                   "gwf0", "gwf1", "gwf2", "gwf3", "gx2", "ghr0", "ghr1", "gxf",
                   "gw30", "gw31", "dq0", "dq1",
                   "qx0", "qx1", "qq0", "qq1", "qc0", "qc1", "qd0", "qd1",
                   "qe0", "qe1", "qo0", "qo1"]
                  + [f"gz{i}" for i in range(8)]):
            sem[s] = E(nc.semaphore(s))
            cnt[s] = 0

        ident = E(nc.sbuf_tensor("ident_s", [128, 128], BF16))
        mmat = E(nc.sbuf_tensor("mmat_s", [128, 128], F32))
        amats = E(nc.sbuf_tensor("amats", [128, H, 128], BF16))
        wsb = E(nc.sbuf_tensor("wsb", [128, H], BF16))
        p_cols = E(nc.sbuf_tensor("p_cols", [1, H * 128], BF16))
        smalls = E(nc.sbuf_tensor("smalls", [128, 48], F32))
        sm16 = E(nc.sbuf_tensor("sm16", [16, 20], F32))
        iota16 = E(nc.sbuf_tensor("iota16_s", [16, 128], F32))
        p16 = E(nc.sbuf_tensor("p16", [16, 128], F32))
        rows127 = E(nc.sbuf_tensor("rows127_s", [128, H], F32))
        cexpo = E(nc.sbuf_tensor("cexpo_s", [16, 8], F32))
        eexpo = E(nc.sbuf_tensor("eexpo_s", [16, 8], F32))
        eye16 = E(nc.sbuf_tensor("eye16_s", [16, 16], F32))
        onesf = E(nc.sbuf_tensor("onesf_s", [128, 128], F32))
        wpart = E(nc.sbuf_tensor("wpart", [128, 24], F32))
        sland = E(nc.sbuf_tensor("sland", [128, 16], F32))
        rs123 = E(nc.sbuf_tensor("rs123", [128, 3, NT], F32))
        exch = E(nc.sbuf_tensor("exch", [128, 260], F32))

        P0 = E(nc.psum_tensor("P0", [128, 2048], F32))
        P1 = E(nc.psum_tensor("P1", [128, 1024], F32))
        P2 = E(nc.psum_tensor("P2", [128, 1024], F32))
        P1b = P1.ap().bitcast(BF16)
        P2b = P2.ap().bitcast(BF16)
        qkv_slabs = [P2.ap()[:, 0:256], P2.ap()[:, 512:768],
                     P1.ap()[:, 0:256], P1.ap()[:, 512:768]]
        sums_ps = [P1.ap()[0:16, 0:512], P1.ap()[0:16, 512:1024]]
        wout_ps = [P0.ap()[:, i * 256:(i + 1) * 256] for i in range(8)]
        ffn_slabs = [P2.ap()[:, 0:512], P2.ap()[:, 512:1024],
                     P1.ap()[:, 0:512], P1.ap()[:, 512:1024]]
        f3_slabs = [P0.ap()[:, i * 512:(i + 1) * 512] for i in range(4)]

        blk = E(nc.Block())
        ld = {}

        with ExitStack() as stAB:
            EA = stAB.enter_context
            arena = EA(nc.sbuf_tensor("arena", [128, 48 * 1024], BF16))
            ar = arena.ap()
            kvk_t = [ar[:, t * 4096:(t + 1) * 4096] for t in range(NT)]
            q_t = [ar[:, 32768 + t * 2048:32768 + (t + 1) * 2048] for t in range(NT)]
            xq1T_t = q_t
            oT_t = [ar[:, t * 4096:t * 4096 + 2048] for t in range(NT)]
            r_t = [ar[:, t * 4096 + 2048:(t + 1) * 4096] for t in range(NT)]
            stAX = ExitStack()
            sums = stAX.enter_context(nc.sbuf_tensor("sums", [16, NT, 256], F32))
            st_in = stAX.enter_context(nc.sbuf_tensor("st_in", [16, NT, 256], F32))

            with ExitStack() as stA:
                ES = stA.enter_context
                xTb = ES(nc.sbuf_tensor("xTb", [128, NDC, TC], BF16))
                wqb = ES(nc.sbuf_tensor("wqb", [128, 2, NDC, 256], BF16))
                phi_s = ES(nc.sbuf_tensor("phi_s", [128, 2, 256], F32))
                exp_s = ES(nc.sbuf_tensor("exp_s", [128, 2, 256], F32))
                strip = ES(nc.sbuf_tensor("strip", [128, 2, 2048], F32))

                @blk.gpsimd
                def _(g):
                    for name, dstp, src in [
                        ("ident", ident[:], ident_in), ("mmat", mmat[:], mmat_in),
                        ("drep", smalls[:, 0:16], drep_in), ("d16", sm16[:, 0:1], d16_in),
                        ("iota16", iota16[:], iota16_in), ("rows127", rows127[:], rows127_in),
                        ("cexpo", cexpo[:], cexpo_in), ("eexpo", eexpo[:], eexpo_in),
                        ("eye16", eye16[:], eye16_in),
                        ("onesf", onesf[:], onesf_in),
                    ]:
                        ld[name] = inc(g.dma_start(out=dstp, in_=src), "gsu", 16)
                    for i in range(4):
                        ld[f"xT{i}"] = inc(g.dma_start(
                            out=xTb[:, i * 4:(i + 1) * 4, :],
                            in_=xT_in.rearrange("(c p) t -> p c t", p=128)[:, i * 4:(i + 1) * 4, :]),
                            "gxT", 16)
                    nstr = 0
                    for wi, srcw in enumerate([ws1_in, ws2_in, ws3_in]):
                        na = srcw.shape[0] // 128
                        nb = srcw.shape[1] // 2048
                        view = srcw.rearrange("(a p) (b n) -> p a b n", p=128, n=2048)
                        for ai in range(na):
                            for bi in range(nb):
                                if nstr >= 2:
                                    wt(g, "pW", nstr - 1)
                                ld[f"st{nstr}"] = inct(g.dma_start(
                                    out=strip[:, nstr % 2, :], in_=view[:, ai, bi, :]),
                                    f"gst{nstr % 2}")
                                nstr += 1
                    for osl in range(24):
                        if osl >= 2:
                            wt(g, "t", 8 * (osl - 1))
                        ld[f"wq{osl}"] = inct(g.dma_start(
                            out=wqb[:, osl % 2, :, :],
                            in_=wqkv_in[:, osl * 256:(osl + 1) * 256]
                                .rearrange("(c p) n -> p c n", p=128)), f"gwq{osl % 2}")

                @blk.scalar
                def _(s):
                    wt(s, "gsu", cnt["gsu"])
                    s.activation(smalls[:, 0:16], smalls[:, 0:16], AF.Sigmoid)
                    s.activation(smalls[:, 16:32], smalls[:, 0:16], AF.Ln)
                    s.activation(sm16[:, 0:1], sm16[:, 0:1], AF.Sigmoid)
                    s.activation(sm16[:, 1:2], sm16[:, 0:1], AF.Ln)
                    s.activation(p16[:], iota16[:], AF.Exp, scale=sm16[:, 1:2])
                    s.activation(sm16[:, 2:10], cexpo[:], AF.Exp, scale=sm16[:, 1:2])
                    ins = s.activation(sm16[:, 10:18], eexpo[:], AF.Exp, scale=sm16[:, 1:2])
                    inc(s.drain(), "s", 1)
                s_setup1 = cnt["s"]

                @blk.vector
                def _(v):
                    v.memset(sland[:], 0.0)
                    wt(v, "gsu", cnt["gsu"])
                    wt(v, "s", s_setup1)
                    ins = v.tensor_tensor(smalls[:, 32:48], rows127[:], smalls[:, 16:32], ALU.mult)
                    inc(ins, "v", 1)
                v_setup1 = cnt["v"]

                @blk.scalar
                def _(s):
                    wt(s, "v", v_setup1)
                    s.activation(wsb[:], smalls[:, 32:48], AF.Exp)
                    for h in range(H):
                        s.activation(amats[:, h, :], mmat[:], AF.Exp,
                                     scale=smalls[:, 16 + h:17 + h])
                    ins = None
                    for nstr in range(24):
                        wtt(s, ld[f"st{nstr}"])
                        ins = s.activation(strip[:, nstr % 2, :], strip[:, nstr % 2, :],
                                           AF.Abs, accum_out=wpart[:, nstr:nstr + 1])
                        if nstr < 23:
                            inc(ins, "pW", 1)
                    inc(s.drain(), "pW", 1)

                @blk.gpsimd
                def _(g):
                    wt(g, "s", s_setup1)
                    inc(g.dma_start(out=pp_d.ap().rearrange("(p n) -> p n", p=16),
                                    in_=p16[:]), "gpc", 16)
                    wt(g, "gpc", 16)
                    ld["p_cols"] = inct(g.dma_start(
                        out=p_cols[:], in_=pp_d.ap().rearrange("(o x) -> o x", o=1)), "gpc")

                tA0 = cnt["t"]

                @blk.tensor
                def _(t):
                    wt(t, "gxT", cnt["gxT"])
                    nslab = 0
                    for osl in range(24):
                        wtt(t, ld[f"wq{osl}"])
                        for tt in range(NT):
                            if nslab >= 4:
                                wt(t, "pA", nslab - 3)
                            ps = qkv_slabs[nslab % 4]
                            ins = None
                            for d in range(NDC):
                                ins = t.matmul(ps, xTb[:, d, tt * 128:(tt + 1) * 128],
                                               wqb[:, osl % 2, d, :],
                                               start=(d == 0), stop=(d == NDC - 1))
                            inc(ins, "t", 1)
                            nslab += 1
                assert cnt["t"] == tA0 + 192

                phi_snap = {}

                @blk.scalar
                def _(s):
                    for k in range(192):
                        osl = k // 8
                        if osl >= 16:
                            continue
                        wt(s, "t", tA0 + k + 1)
                        if k >= 2:
                            wt(s, "pA", k - 1)
                        ps = qkv_slabs[k % 4]
                        s.activation(phi_s[:, k % 2, :], ps, AF.Relu, scale=-1.0)
                        ins = s.activation(exp_s[:, k % 2, :], phi_s[:, k % 2, :],
                                           AF.Exp, scale=-1.0)
                        phi_snap[k] = inc(ins, "s", 1)

                @blk.vector
                def _(v):
                    for k in range(192):
                        osl, tt = k // 8, k % 8
                        ps = qkv_slabs[k % 4]
                        if osl < 8:
                            wt(v, "s", phi_snap[k])
                            ins = v.scalar_tensor_tensor(
                                q_t[tt][:, osl * 256:(osl + 1) * 256], ps, 0.0,
                                exp_s[:, k % 2, :], ALU.max, ALU.add)
                        elif osl < 16:
                            wt(v, "s", phi_snap[k])
                            h0 = (osl - 8) * 2
                            kview = kvk_t[tt].rearrange("p (h two x) -> p h two x", two=2, x=128)
                            ins = v.scalar_tensor_tensor(
                                kview[:, h0:h0 + 2, 1, :], ps, 0.0,
                                exp_s[:, k % 2, :], ALU.max, ALU.add)
                        else:
                            wt(v, "t", tA0 + k + 1)
                            h0 = (osl - 16) * 2
                            kview = kvk_t[tt].rearrange("p (h two x) -> p h two x", two=2, x=128)
                            ins = v.tensor_tensor(kview[:, h0:h0 + 2, 0, :], ps,
                                                  kview[:, h0:h0 + 2, 1, :], ALU.mult)
                        inc(ins, "pA", 1)
                assert cnt["pA"] == 192

                tA2 = cnt["t"]

                @blk.tensor
                def _(t):
                    wt(t, "pA", 192)
                    for k in range(64):
                        tt, j8 = k // 8, k % 8
                        if k >= 2:
                            wt(t, "pS", k - 1)
                        ins = t.matmul(sums_ps[k % 2], wsb[:],
                                       kvk_t[tt][:, j8 * 512:(j8 + 1) * 512],
                                       start=True, stop=True)
                        inc(ins, "t", 1)

                @blk.vector
                def _(v):
                    wt(v, "gsu", cnt["gsu"])
                    for k in range(64):
                        tt, j8 = k // 8, k % 8
                        wt(v, "t", tA2 + k + 1)
                        ps = sums_ps[k % 2]
                        if j8 == 0:
                            v.tensor_scalar(sums[:, tt, :], ps[:, 0:256],
                                            eye16[:, 0:1], None, ALU.mult)
                        else:
                            v.scalar_tensor_tensor(sums[:, tt, :], ps[:, 0:256],
                                                   eye16[:, 2 * j8:2 * j8 + 1],
                                                   sums[:, tt, :], ALU.mult, ALU.add)
                        v.drain()
                        ins = v.scalar_tensor_tensor(sums[:, tt, :], ps[:, 256:512],
                                                     eye16[:, 2 * j8 + 1:2 * j8 + 2],
                                                     sums[:, tt, :], ALU.mult, ALU.add)
                        v.drain()
                        inc(ins, "pS", 1)
                sums_done = cnt["pS"]

                @blk.vector
                def _(v):
                    wt(v, "pS", sums_done)
                    wt(v, "pW", 24)
                    v.memset(exch[:, :], 0.0)
                    v.tensor_scalar(exch[0:16, 0:256], sums[:, 0, :], sm16[:, 2:3], None, ALU.mult)
                    ins = None
                    for c in range(1, 8):
                        v.drain()
                        ins = v.scalar_tensor_tensor(exch[0:16, 0:256], sums[:, c, :],
                                                     sm16[:, 2 + c:3 + c], exch[0:16, 0:256],
                                                     ALU.mult, ALU.add)
                    v.drain()
                    for wi in range(3):
                        ins = v.reduce_sum(exch[:, 256 + wi:257 + wi],
                                           wpart[:, wi * 8:(wi + 1) * 8], axis=AX.X)
                    inc(v.drain(), "v", 1)
                v_exch = cnt["v"]

            with ExitStack() as stX:
                EX = stX.enter_context
                G = EX(nc.sbuf_tensor("G", [16, 8, 256], F32))
                wG = EX(nc.sbuf_tensor("wG", [128, 8, 3], F32))

                @blk.gpsimd
                def _(g):
                    wt(g, "v", v_exch)
                    inc(g.dma_start(out=cc_in.ap(), in_=exch[:]), "gcc", 16)
                    wt(g, "gcc", 16)
                    ins = g.collective_compute(
                        "AllGather", ALU.bypass, replica_groups=[list(range(NCORE))],
                        ins=[cc_in.ap().opt()], outs=[cc_out.ap().opt()])
                    inc(ins, "cc", 1)
                    wt(g, "cc", 1)
                    for j in range(8):
                        ld[f"G{j}"] = inct(g.dma_start(
                            out=G[:, j, :], in_=cc_out.ap()[j, 0:16, 0:256]), "gG")
                    ld["wG"] = inct(g.dma_start(
                        out=wG[:], in_=cc_out.ap()[:, :, 256:259].rearrange("j p c -> p j c")),
                        "gG")

                @blk.vector
                def _(v):
                    wt(v, "gG", cnt["gG"])
                    ins = None
                    for wi in range(3):
                        ins = v.reduce_sum(sland[:, 9 + wi:10 + wi],
                                           wG.ap().rearrange("p j c -> p c j")[:, wi, :],
                                           axis=AX.X)
                    inc(v.drain(), "v", 1)
                v_wsum = cnt["v"]

                @blk.tensor
                def _(t):
                    wt(t, "v", v_wsum)
                    wt(t, "gsu", cnt["gsu"])
                    ins = t.matmul(P2.ap()[0:1, 0:3], onesf[:, 0:1], sland[:, 9:12],
                                   start=True, stop=True)
                    inc(ins, "t", 1)
                t_wr1 = cnt["t"]

                @blk.scalar
                def _(s):
                    wt(s, "t", t_wr1)
                    ins = s.copy(sland[0:1, 12:15], P2.ap()[0:1, 0:3])
                    inc(s.drain(), "s", 1)
                s_wr = cnt["s"]

                @blk.tensor
                def _(t):
                    wt(t, "s", s_wr)
                    ins = t.matmul(P1.ap()[:, 0:3], onesf[0:1, :], sland[0:1, 12:15],
                                   start=True, stop=True)
                    inc(ins, "t", 1)
                t_wr2 = cnt["t"]

                @blk.vector
                def _(v):
                    wt(v, "t", t_wr2)
                    v.tensor_scalar(sland[:, 0:3], P1.ap()[:, 0:3], 1.0 / NW_ELEM, None, ALU.mult)
                    v.drain()
                    v.tensor_scalar(smalls[:, 32:35], sland[:, 0:3], 1e-8, 2.0 * ATANH_HALF,
                                    ALU.add, ALU.mult)
                    v.drain()
                    v.reciprocal(sland[:, 3:6], smalls[:, 32:35])
                    v.tensor_scalar(smalls[:, 35:38], sland[:, 0:3], -1.0, 1.0, ALU.mult, ALU.add)
                    v.drain()
                    v.reciprocal(sland[:, 12:15], smalls[:, 35:38])
                    v.drain()
                    v.tensor_scalar(sland[:, 9:12], sland[:, 0:3], 1.0, None, ALU.add)
                    v.drain()
                    v.tensor_tensor(smalls[:, 38:41], sland[:, 9:12], sland[:, 12:15], ALU.mult)
                    inc(v.drain(), "v", 1)
                v_sm1 = cnt["v"]

                @blk.scalar
                def _(s):
                    wt(s, "v", v_sm1)
                    ins = s.activation(sland[:, 9:12], smalls[:, 38:41], AF.Ln)
                    inc(s.drain(), "s", 1)
                s_ln = cnt["s"]

                @blk.vector
                def _(v):
                    wt(v, "s", s_ln)
                    v.tensor_scalar(sland[:, 6:9], sland[:, 9:12], 0.5, None, ALU.mult)
                    wt(v, "gG", cnt["gG"])
                    v.tensor_scalar(st_in[:, 0, :], G[:, 0, :], sm16[:, 10:11], None, ALU.mult)
                    for j in range(1, 8):
                        v.drain()
                        v.scalar_tensor_tensor(st_in[:, 0, :], G[:, j, :], sm16[:, 10 + j:11 + j],
                                               st_in[:, 0, :], ALU.mult, ALU.add)
                    ins = None
                    for c in range(1, 8):
                        v.drain()
                        ins = v.scalar_tensor_tensor(st_in[:, c, :], st_in[:, c - 1, :],
                                                     sm16[:, 8:9], sums[:, c - 1, :],
                                                     ALU.mult, ALU.add)
                    inc(v.drain(), "v", 1)
                v_chain = cnt["v"]

                @blk.gpsimd
                def _(g):
                    wt(g, "v", v_chain)
                    for c in range(8):
                        ld[f"stw{c}"] = inct(g.dma_start(
                            out=strow_d.ap()[c].rearrange("(p n) -> p n", p=16),
                            in_=st_in[:, c, :]), "gstw")

            if dbg_t:
                @blk.sync
                def _(sy):
                    if "dbg_q" in dbg_t:
                        wt(sy, "pA", 192)
                        for tt in range(NT):
                            inc(sy.dma_start(out=dbg_t["dbg_q"][:, tt, :], in_=q_t[tt]), "q", 16)
                    if "dbg_kvk" in dbg_t:
                        wt(sy, "pA", 192)
                        for tt in range(NT):
                            inc(sy.dma_start(out=dbg_t["dbg_kvk"][:, tt, :], in_=kvk_t[tt]), "q", 16)
                    if "dbg_sums" in dbg_t:
                        wt(sy, "pS", sums_done)
                        inc(sy.dma_start(out=dbg_t["dbg_sums"], in_=sums[:]), "q", 16)
                    if "dbg_stin" in dbg_t:
                        wt(sy, "v", v_chain)
                        inc(sy.dma_start(out=dbg_t["dbg_stin"], in_=st_in[:]), "q", 16)
                    if "dbg_sw" in dbg_t:
                        wt(sy, "v", v_chain)
                        inc(sy.dma_start(out=dbg_t["dbg_sw"], in_=sland[:]), "q", 16)

            def finish_zero_out():
                @blk.vector
                def _(v):
                    ins = v.memset(exch[:, 0:256], 0.0)
                    inc(ins, "v", 1)
                vz = cnt["v"]

                @blk.sync
                def _(sy):
                    wt(sy, "v", vz)
                    for tt in range(NT):
                        for c in range(8):
                            inc(sy.dma_start(
                                out=out_t[tt * 128:(tt + 1) * 128, c * 256:(c + 1) * 256],
                                in_=exch[:, 0:256]), "q", 16)

                @blk.scalar
                def _(s):
                    for n_ in cnt:
                        if (n_[0] == "q" or n_[:2] == "dq") and cnt[n_] > 0:
                            wt(s, n_, cnt[n_])

            if stage == "ax":
                finish_zero_out()
                return nc

            stAX.close()
            with ExitStack() as stB:
                EB = stB.enter_context
                st_row = EB(nc.sbuf_tensor("st_row", [1, 4096], BF16))
                o_ts = EB(nc.sbuf_tensor("o_ts", [128, 2, D], BF16))
                den = EB(nc.sbuf_tensor("den", [128, 2, 16], F32))
                dentmp = EB(nc.sbuf_tensor("dentmp", [128, 2, 16], F32))
                rcpB = EB(nc.sbuf_tensor("rcpB", [128, NT, 4], F32))
                rden = EB(nc.sbuf_tensor("rden", [128, 2, 16], F32))
                dmp = EB(nc.sbuf_tensor("dmp", [128, 128], F32))
                wob = EB(nc.sbuf_tensor("wob", [128, 2, NDC, 128], BF16))
                xsl = EB(nc.sbuf_tensor("xsl", [128, 2, NT, 128], F32))
                rpart = EB(nc.sbuf_tensor("rpart", [128, NT, 16], F32))
                bst = EB(nc.sbuf_tensor("bst", [128, NT, 8], F32))
                qm = EB(nc.sbuf_tensor("qm", [128, NT, 8], F32))
                g1rb = EB(nc.sbuf_tensor("g1rb", [128, D], F32))
                g2rb = EB(nc.sbuf_tensor("g2rb", [128, D], F32))
                x1_st = EB(nc.sbuf_tensor("x1_st", [128, 2, D], BF16))
                x1f = EB(nc.sbuf_tensor("x1f", [128, D], F32))
                u_t = EB(nc.sbuf_tensor("u_t", [128, D], F32))
                xq_st = EB(nc.sbuf_tensor("xq_st", [128, 2, 2, D], BF16))
                q2st = EB(nc.sbuf_tensor("q2st", [128, 2, D], BF16))

                @blk.gpsimd
                def _(g):
                    wt(g, "gstw", cnt["gstw"])
                    for c in range(8):
                        if c >= 1:
                            wt(g, "pB", 2 * c)
                        ld[f"str{c}"] = inct(g.dma_start(
                            out=st_row[:],
                            in_=strow_d.ap()[c].rearrange("(o x) -> o x", o=1)), "gsr")
                    ld["g1"] = inct(g.dma_start(out=g1rb[:], in_=g1r_in), "gg12")
                    ld["g2"] = inct(g.dma_start(out=g2rb[:], in_=g2r_in), "gg12")
                    for osl in range(16):
                        if osl >= 2:
                            wt(g, "pB3", 8 * (osl - 1))
                            wt(g, "pR", 8 * (osl - 1))
                        ld[f"wo{osl}"] = inct(g.dma_start(
                            out=wob[:, osl % 2, :, :],
                            in_=wout_in[:, osl * 128:(osl + 1) * 128]
                                .rearrange("(c p) n -> p c n", p=128)), f"gwo{osl % 2}")
                        ld[f"xs{osl}"] = inct(g.dma_start(
                            out=xsl[:, osl % 2, :, :],
                            in_=x_in[:, osl * 128:(osl + 1) * 128]
                                .rearrange("(tt p) n -> p tt n", p=128)), f"gxs{osl % 2}")

                scan_snap = {}
                tpo_snap = {}

                @blk.tensor
                def _(t):
                    wtt(t, ld["p_cols"])
                    wt(t, "gsu", cnt["gsu"])
                    for tt in range(NT):
                        wtt(t, ld[f"str{tt}"])
                        for grp in range(2):
                            gidx = 2 * tt + grp
                            wt(t, "pO", gidx)
                            ins = None
                            for hh in range(8):
                                h = grp * 8 + hh
                                t.matmul(P0.ap()[:, hh * 256:(hh + 1) * 256],
                                         amats[:, h, :], kvk_t[tt][:, h * 256:(h + 1) * 256],
                                         start=True, stop=False)
                                ins = t.matmul(P0.ap()[:, hh * 256:(hh + 1) * 256],
                                               p_cols[:, h * 128:(h + 1) * 128],
                                               st_row[:, h * 256:(h + 1) * 256],
                                               start=False, stop=True)
                            scan_snap[gidx] = inc(ins, "pB", 1)
                        if tt > 0:
                            wt(t, "pT1", tt - 1)
                            ins = None
                            for d in range(NDC):
                                ins = t.transpose(P1b[:, d * 128:(d + 1) * 128],
                                                  o_ts[:, (tt - 1) % 2, d * 128:(d + 1) * 128],
                                                  ident[:])
                            tpo_snap[tt - 1] = inc(ins, "pB2", 1)
                    wt(t, "pO", 2 * NT)
                    wt(t, "pT1", NT - 1)
                    ins = None
                    for d in range(NDC):
                        ins = t.transpose(P1b[:, d * 128:(d + 1) * 128],
                                          o_ts[:, (NT - 1) % 2, d * 128:(d + 1) * 128], ident[:])
                    tpo_snap[NT - 1] = inc(ins, "pB2", 1)

                @blk.scalar
                def _(s):
                    for tt in range(NT):
                        wt(s, "pB2", tpo_snap[tt])
                        ins = s.copy(oT_t[tt], P1b[:, 0:2048])
                        inc(s.drain(), "pT1", 1)

                @blk.vector
                def _(v):
                    for tt in range(NT):
                        if tt >= 2:
                            wt(v, "pB2", tpo_snap[tt - 2])
                        for grp in range(2):
                            gidx = 2 * tt + grp
                            wt(v, "pB", scan_snap[gidx])
                            for hh in range(8):
                                h = grp * 8 + hh
                                v.scalar_tensor_tensor(
                                    dmp[:], q_t[tt][:, h * 128:(h + 1) * 128], 1.0,
                                    P0.ap()[:, hh * 256 + 128:hh * 256 + 256],
                                    ALU.mult, ALU.mult,
                                    accum_out=den[:, tt % 2, h:h + 1])
                            v.drain()
                            v.tensor_scalar(dentmp[:, tt % 2, grp * 8:(grp + 1) * 8],
                                            den[:, tt % 2, grp * 8:(grp + 1) * 8],
                                            1e-6, None, ALU.max)
                            v.reciprocal(rden[:, tt % 2, grp * 8:(grp + 1) * 8],
                                         dentmp[:, tt % 2, grp * 8:(grp + 1) * 8])
                            v.reciprocal(rden[:, tt % 2, grp * 8:(grp + 1) * 8],
                                         dentmp[:, tt % 2, grp * 8:(grp + 1) * 8])
                            v.drain()
                            ins = None
                            for hh in range(8):
                                h = grp * 8 + hh
                                ins = v.scalar_tensor_tensor(
                                    o_ts[:, tt % 2, h * 128:(h + 1) * 128],
                                    q_t[tt][:, h * 128:(h + 1) * 128],
                                    rden[:, tt % 2, h:h + 1],
                                    P0.ap()[:, hh * 256:hh * 256 + 128],
                                    ALU.mult, ALU.mult)
                            inc(v.drain(), "pO", 1)
                assert cnt["pO"] == 2 * NT

                tB2_snap = {}

                @blk.tensor
                def _(t):
                    wt(t, "pT1", NT)
                    for osl in range(16):
                        wtt(t, ld[f"wo{osl}"])
                        for tt in range(NT):
                            sidx = osl * NT + tt
                            if sidx >= 4:
                                wt(t, "pR", sidx - 3)
                            ins = None
                            for d in range(NDC):
                                ins = t.matmul(
                                    qkv_slabs[sidx % 4][:, 0:128],
                                    oT_t[tt][:, d * 128:(d + 1) * 128],
                                    wob[:, osl % 2, d, :],
                                    start=(d == 0), stop=(d == NDC - 1))
                            tB2_snap[sidx] = inc(ins, "pB3", 1)

                @blk.vector
                def _(v):
                    for osl in range(16):
                        wtt(v, ld[f"xs{osl}"])
                        for tt in range(NT):
                            sidx = osl * NT + tt
                            wt(v, "pB3", tB2_snap[sidx])
                            ins = v.scalar_tensor_tensor(
                                r_t[tt][:, osl * 128:(osl + 1) * 128],
                                qkv_slabs[sidx % 4][:, 0:128],
                                1.0, xsl[:, osl % 2, tt, :],
                                ALU.mult, ALU.add,
                                accum_out=rpart[:, tt, osl:osl + 1])
                            inc(ins, "pR", 1)
                assert cnt["pR"] == 128

                @blk.vector
                def _(v):
                    wt(v, "pR", 128)
                    v.drain()
                    ins = None
                    for tt in range(NT):
                        v.reduce_sum(bst[:, tt, 0:1], rpart[:, tt, :], axis=AX.X)
                    v.drain()
                    for tt in range(NT):
                        ins = v.tensor_scalar(bst[:, tt, 1:2], bst[:, tt, 0:1],
                                              1.0 / D, None, ALU.mult)
                    v.drain()
                    inc(v.drain(), "v", 1)
                vB3a = cnt["v"]

                @blk.scalar
                def _(s):
                    wt(s, "v", vB3a)
                    ins = None
                    for tt in range(NT):
                        s.activation(bst[:, tt, 3:4], bst[:, tt, 1:2], AF.Square)
                        ins = s.activation(u_t[:, 0:2048], r_t[tt], AF.Square,
                                           accum_out=bst[:, tt, 2:3])
                    inc(s.drain(), "s", 1)
                sB3b = cnt["s"]

                @blk.vector
                def _(v):
                    wt(v, "s", sB3b)
                    ins = None
                    for tt in range(NT):
                        v.scalar_tensor_tensor(bst[:, tt, 4:5], bst[:, tt, 2:3], 1.0 / D,
                                               bst[:, tt, 3:4], ALU.mult, ALU.subtract)
                        v.drain()
                        v.tensor_scalar(bst[:, tt, 4:5], bst[:, tt, 4:5], 1e-5, None, ALU.add)
                        v.drain()
                        v.reciprocal(rcpB[:, tt, 3:4], bst[:, tt, 4:5])
                        v.drain()
                        ins = v.tensor_scalar(bst[:, tt, 5:6], rcpB[:, tt, 3:4],
                                              -1e-5, 1.0 + 1e-8, ALU.mult, ALU.add)
                    inc(v.drain(), "v", 1)
                vB3c = cnt["v"]

                @blk.scalar
                def _(s):
                    wt(s, "v", vB3c)
                    ins = None
                    for tt in range(NT):
                        s.activation(bst[:, tt, 6:7], bst[:, tt, 4:5], AF.Sqrt)
                        ins = s.activation(bst[:, tt, 7:8], bst[:, tt, 5:6], AF.Sqrt)
                    inc(s.drain(), "s", 1)
                sB3d = cnt["s"]

                vq_snap = {}

                @blk.vector
                def _(v):
                    wt(v, "s", sB3d)
                    wt(v, "gg12", cnt["gg12"])
                    for tt in range(NT):
                        if tt >= 2:
                            wt(v, f"qx{tt % 2}", 16 * (tt // 2))
                            wt(v, "pB4", 2 * (tt - 1))
                        v.reciprocal(rcpB[:, tt, 0:1], bst[:, tt, 6:7])
                        v.reciprocal(rcpB[:, tt, 0:1], bst[:, tt, 6:7])
                        v.reciprocal(rcpB[:, tt, 1:2], bst[:, tt, 7:8])
                        v.reciprocal(rcpB[:, tt, 1:2], bst[:, tt, 7:8])
                        v.tensor_scalar(x1f[:], r_t[tt],
                                        bst[:, tt, 1:2], rcpB[:, tt, 0:1],
                                        ALU.subtract, ALU.mult)
                        v.tensor_copy(x1_st[:, tt % 2, :], x1f[:])
                        for wch, grb in ((0, g1rb), (1, g2rb)):
                            v.tensor_tensor(u_t[:], x1f[:], grb[:], ALU.mult)
                            v.reduce_max(qm[:, tt, 4 * wch:4 * wch + 1], u_t[:],
                                         axis=AX.X, apply_absolute_value=True)
                            v.drain()
                            v.tensor_scalar(qm[:, tt, 4 * wch + 1:4 * wch + 2],
                                            qm[:, tt, 4 * wch:4 * wch + 1],
                                            rcpB[:, tt, 1:2], 1e-4, ALU.mult, ALU.max)
                            v.drain()
                            v.reciprocal(rcpB[:, tt, 2 + wch:3 + wch],
                                         qm[:, tt, 4 * wch + 1:4 * wch + 2])
                            v.drain()
                            v.scalar_tensor_tensor(qm[:, tt, 4 * wch + 3:4 * wch + 4],
                                                   rcpB[:, tt, 2 + wch:3 + wch], 127.0,
                                                   rcpB[:, tt, 1:2], ALU.mult, ALU.mult)
                            v.drain()
                            v.tensor_scalar(u_t[:], u_t[:],
                                            qm[:, tt, 4 * wch + 3:4 * wch + 4], MAGIC,
                                            ALU.mult, ALU.add)
                            v.tensor_scalar(xq_st[:, wch, tt % 2, :], u_t[:],
                                            -MAGIC, None, ALU.add)
                            v.scalar_tensor_tensor(
                                rs123[:, wch, tt:tt + 1],
                                qm[:, tt, 4 * wch + 1:4 * wch + 2], 1.0 / 127.0,
                                sland[:, 6 + wch:7 + wch], ALU.mult, ALU.mult)
                            inc(v.drain(), "pQ", 1)
                        vq_snap[tt] = cnt["pQ"]
                assert cnt["pQ"] == 2 * NT

                tq_snap = {}

                @blk.tensor
                def _(t):
                    wt(t, "gsu", cnt["gsu"])
                    for tt in range(NT):
                        wt(t, "pQ", 2 * tt + 1)
                        if tt >= 1:
                            wt(t, "pT3", 2 * tt - 1)
                        ins = None
                        for d in range(NDC):
                            ins = t.transpose(P1b[:, d * 128:(d + 1) * 128],
                                              xq_st[:, 0, tt % 2, d * 128:(d + 1) * 128],
                                              ident[:])
                        tq_snap[(tt, 0)] = inc(ins, "pB4", 1)
                        wt(t, "pQ", 2 * tt + 2)
                        if tt >= 1:
                            wt(t, "pT3", 2 * tt)
                        ins = None
                        for d in range(NDC):
                            ins = t.transpose(P2b[:, d * 128:(d + 1) * 128],
                                              xq_st[:, 1, tt % 2, d * 128:(d + 1) * 128],
                                              ident[:])
                        tq_snap[(tt, 1)] = inc(ins, "pB4", 1)

                @blk.scalar
                def _(s):
                    for tt in range(NT):
                        wt(s, "pB4", tq_snap[(tt, 0)])
                        ins = s.copy(xq1T_t[tt], P1b[:, 0:2048])
                        inc(s.drain(), "pT3", 1)
                        wt(s, "pB4", tq_snap[(tt, 1)])
                        if tt >= 2:
                            wt(s, f"qq{tt % 2}", 16 * (tt // 2))
                        ins = s.copy(q2st[:, tt % 2, :], P2b[:, 0:2048])
                        inc(s.drain(), "pT3", 1)

                @blk.sync
                def _(sy):
                    for tt in range(NT):
                        wt(sy, "pQ", vq_snap[tt])
                        inc(sy.dma_start(out=spl_x1.ap()[tt], in_=x1_st[:, tt % 2, :]),
                            f"qx{tt % 2}", 16)
                        wt(sy, "pT3", 2 * tt + 2)
                        inc(sy.dma_start(out=spl_q2T.ap()[tt], in_=q2st[:, tt % 2, :]),
                            f"qq{tt % 2}", 16)
                qB_tot = {n: cnt[n] for n in ("qx0", "qx1", "qq0", "qq1")}

                if dbg_t:
                    @blk.sync
                    def _(sy):
                        if "dbg_o" in dbg_t:
                            wt(sy, "pT1", NT)
                            for tt in range(NT):
                                inc(sy.dma_start(out=dbg_t["dbg_o"][tt], in_=oT_t[tt]), "q", 16)
                        if "dbg_r" in dbg_t:
                            wt(sy, "pR", 128)
                            for tt in range(NT):
                                inc(sy.dma_start(out=dbg_t["dbg_r"][tt], in_=r_t[tt]), "q", 16)
                        if "dbg_xq1" in dbg_t:
                            wt(sy, "pT3", 2 * NT)
                            for tt in range(NT):
                                inc(sy.dma_start(out=dbg_t["dbg_xq1"][tt], in_=xq1T_t[tt]), "q", 16)
                        if "dbg_rs" in dbg_t:
                            wt(sy, "pQ", 2 * NT)
                            inc(sy.dma_start(out=dbg_t["dbg_rs"], in_=rs123[:]), "q", 16)
                        if "dbg_bst" in dbg_t:
                            wt(sy, "pQ", 2 * NT)
                            inc(sy.dma_start(out=dbg_t["dbg_bst"], in_=bst[:]), "q", 16)
                        if "dbg_qm" in dbg_t:
                            wt(sy, "pQ", 2 * NT)
                            inc(sy.dma_start(out=dbg_t["dbg_qm"], in_=qm[:]), "q", 16)
                        if "dbg_rcpB" in dbg_t:
                            wt(sy, "pQ", 2 * NT)
                            inc(sy.dma_start(out=dbg_t["dbg_rcpB"], in_=rcpB[:]), "q", 16)

                if stage == "b":
                    finish_zero_out()
                    return nc

            with ExitStack() as stC:
                EC = stC.enter_context
                wf = EC(nc.sbuf_tensor("wf", [128, 4, 512], F32))
                tmf = EC(nc.sbuf_tensor("tmf", [128, 2, 512], F32))
                w1q = EC(nc.sbuf_tensor("w1q", [128, 2, NDC, 512], BF16))
                z1st = EC(nc.sbuf_tensor("z1st", [128, 2, 512], BF16))
                f1st = EC(nc.sbuf_tensor("f1st", [128, 2, 512], F32)) \
                    if "dbg_f1" in dbg_t else None
                dqC_str = 2 * 16 * (("dbg_z1" in dbg_t) + ("dbg_f1" in dbg_t))

                ldC = {}

                @blk.gpsimd
                def _(g):
                    for n_, v_ in qB_tot.items():
                        wt(g, n_, v_)
                    nchunk = 0
                    for osl in range(16):
                        for d in range(NDC):
                            if nchunk >= 4:
                                wt(g, "pC2", nchunk - 3)
                            ldC[nchunk] = inct(g.dma_start(
                                out=wf[:, nchunk % 4, :],
                                in_=w1T_in[d * 128:(d + 1) * 128,
                                           osl * 512:(osl + 1) * 512]), f"gwf{nchunk % 4}")
                            nchunk += 1

                pB5_C0 = cnt["pB5"]

                @blk.vector
                def _(v):
                    nchunk = 0
                    for osl in range(16):
                        if osl >= 2:
                            wt(v, "pB5", pB5_C0 + 8 * (osl - 1))
                        for d in range(NDC):
                            wtt(v, ldC[nchunk])
                            v.tensor_scalar(tmf[:, nchunk % 2, :], wf[:, nchunk % 4, :],
                                            sland[:, 3:4], MAGIC, ALU.mult, ALU.add)
                            v.tensor_scalar(tmf[:, nchunk % 2, :], tmf[:, nchunk % 2, :],
                                            -MAGIC, 1.0, ALU.add, ALU.min)
                            ins = v.tensor_scalar(w1q[:, osl % 2, d, :],
                                                  tmf[:, nchunk % 2, :],
                                                  -1.0, None, ALU.max)
                            inc(v.drain(), "pC2", 1)
                            nchunk += 1

                tsnapC = {}

                @blk.tensor
                def _(t):
                    wt(t, "pT3", 2 * NT)
                    idx = 0
                    for osl in range(16):
                        wt(t, "pC2", 16 * (osl + 1))
                        for tt in range(NT):
                            if idx >= 4:
                                wt(t, "pC", idx - 3)
                            ins = None
                            for d in range(NDC):
                                ins = t.matmul(ffn_slabs[idx % 4],
                                               xq1T_t[tt][:, d * 128:(d + 1) * 128],
                                               w1q[:, osl % 2, d, :],
                                               start=(d == 0), stop=(d == NDC - 1))
                            tsnapC[idx] = inc(ins, "pB5", 1)
                            idx += 1


                dqC_base = cnt["dq"]

                @blk.scalar
                def _(s):
                    idx = 0
                    for osl in range(16):
                        for tt in range(NT):
                            wt(s, "pB5", tsnapC[idx])
                            if idx >= 2:
                                wt(s, f"qc{idx % 2}", 16 * (idx // 2))
                                if dqC_str:
                                    wt(s, f"dq{idx % 2}", (dqC_str // 2) * (idx // 2))
                            ins = s.activation(z1st[:, idx % 2, :], ffn_slabs[idx % 4],
                                               AF.Silu, scale=rs123[:, 0, tt:tt + 1])
                            if f1st is not None:
                                ins = s.copy(f1st[:, idx % 2, :], ffn_slabs[idx % 4])
                            inc(s.drain(), "pC", 1)
                            idx += 1

                @blk.sync
                def _(sy):
                    idx = 0
                    for osl in range(16):
                        for tt in range(NT):
                            wt(sy, "pC", idx + 1)
                            inc(sy.dma_start(
                                out=spl_z1.ap()[tt, :, osl * 512:(osl + 1) * 512],
                                in_=z1st[:, idx % 2, :]), f"qc{idx % 2}", 16)
                            if "dbg_z1" in dbg_t:
                                inc(sy.dma_start(
                                    out=dbg_t["dbg_z1"][tt, :, osl * 512:(osl + 1) * 512],
                                    in_=z1st[:, idx % 2, :]), f"dq{idx % 2}", 16)
                            if f1st is not None:
                                inc(sy.dma_start(
                                    out=dbg_t["dbg_f1"][tt, :, osl * 512:(osl + 1) * 512],
                                    in_=f1st[:, idx % 2, :]), f"dq{idx % 2}", 16)
                            idx += 1
                qC_tot = {n: cnt[n] for n in ("qc0", "qc1")}

        with ExitStack() as stD:
            ED = stD.enter_context
            xq2Tr = ED(nc.sbuf_tensor("xq2Tr", [128, NT, D], BF16))
            wfD = ED(nc.sbuf_tensor("wfD", [128, 4, 512], F32))
            tmfD = ED(nc.sbuf_tensor("tmfD", [128, 2, 512], F32))
            w2q = ED(nc.sbuf_tensor("w2q", [128, 2, NDC, 512], BF16))
            z1rd = ED(nc.sbuf_tensor("z1rd", [128, 8, 512], BF16))
            hst = ED(nc.sbuf_tensor("hst", [128, 2, 512], BF16))

            ldD = {}

            pB6_D0 = cnt["pB6"]
            pD0 = cnt["pD"]

            @blk.gpsimd
            def _(g):
                for n_, v_ in qC_tot.items():
                    wt(g, n_, v_)
                for tt in range(NT):
                    ldD[f"x2{tt}"] = inc(g.dma_start(out=xq2Tr[:, tt, :],
                                                     in_=spl_q2T.ap()[tt]), "gx2", 16)
                nchunk = 0
                idx = 0
                for osl in range(16):
                    for d in range(NDC):
                        if nchunk >= 4:
                            wt(g, "pF2", nchunk - 3)
                        ldD[nchunk] = inct(g.dma_start(
                            out=wfD[:, nchunk % 4, :],
                            in_=w2T_in[d * 128:(d + 1) * 128, osl * 512:(osl + 1) * 512]),
                            f"gwf{nchunk % 4}")
                        nchunk += 1
                    for tt in range(NT):
                        if idx >= 8:
                            wt(g, "pD", pD0 + idx - 7)
                        ldD[f"z{idx}"] = inct(g.dma_start(
                            out=z1rd[:, idx % 8, :],
                            in_=spl_z1.ap()[tt, :, osl * 512:(osl + 1) * 512]), f"gz{idx % 8}")
                        idx += 1

            d_slabs = f3_slabs + ffn_slabs
            tsnapD = {}

            dqD_base2 = {0: cnt["dq0"], 1: cnt["dq1"]}

            @blk.vector
            def _(v):

                def evac_osl(osl):
                    for tt in range(NT):
                        idx = osl * NT + tt
                        wt(v, "pB6", pB6_D0 + idx + 1)
                        wtt(v, ldD[f"z{idx}"])
                        if idx >= 2:
                            wt(v, f"qd{idx % 2}", 16 * (idx // 2))
                            if "dbg_h" in dbg_t:
                                wt(v, f"dq{idx % 2}", dqD_base2[idx % 2] + 16 * (idx // 2))
                        ins = v.scalar_tensor_tensor(
                            hst[:, idx % 2, :], d_slabs[idx % 8],
                            rs123[:, 1, tt:tt + 1], z1rd[:, idx % 8, :],
                            ALU.mult, ALU.mult)
                        inc(v.drain(), "pD", 1)

                nchunk = 0
                for osl in range(16):
                    if osl >= 2:
                        wt(v, "pB6", pB6_D0 + 8 * (osl - 1))
                    for d in range(NDC):
                        wtt(v, ldD[nchunk])
                        v.tensor_scalar(tmfD[:, nchunk % 2, :], wfD[:, nchunk % 4, :],
                                        sland[:, 4:5], MAGIC, ALU.mult, ALU.add)
                        v.tensor_scalar(tmfD[:, nchunk % 2, :], tmfD[:, nchunk % 2, :],
                                        -MAGIC, 1.0, ALU.add, ALU.min)
                        ins = v.tensor_scalar(w2q[:, osl % 2, d, :], tmfD[:, nchunk % 2, :],
                                              -1.0, None, ALU.max)
                        inc(v.drain(), "pF2", 1)
                        nchunk += 1
                    if osl >= 1:
                        evac_osl(osl - 1)
                evac_osl(15)
            pF2_D_end = cnt["pF2"]

            @blk.tensor
            def _(t):
                wt(t, "gx2", cnt["gx2"])
                idx = 0
                for osl in range(16):
                    wt(t, "pF2", 16 * (osl + 1))
                    for tt in range(NT):
                        if idx >= 8:
                            wt(t, "pD", pD0 + idx - 7)
                        ins = None
                        for d in range(NDC):
                            ins = t.matmul(d_slabs[idx % 8],
                                           xq2Tr[:, tt, d * 128:(d + 1) * 128],
                                           w2q[:, osl % 2, d, :],
                                           start=(d == 0), stop=(d == NDC - 1))
                        tsnapD[idx] = inc(ins, "pB6", 1)
                        idx += 1

            @blk.sync
            def _(sy):
                idx = 0
                for osl in range(16):
                    for tt in range(NT):
                        wt(sy, "pD", pD0 + idx + 1)
                        inc(sy.dma_start(out=spl_h.ap()[tt, :, osl * 512:(osl + 1) * 512],
                                         in_=hst[:, idx % 2, :]), f"qd{idx % 2}", 16)
                        if "dbg_h" in dbg_t:
                            inc(sy.dma_start(
                                out=dbg_t["dbg_h"][tt, :, osl * 512:(osl + 1) * 512],
                                in_=hst[:, idx % 2, :]), f"dq{idx % 2}", 16)
                        idx += 1
            qD_tot = {n: cnt[n] for n in ("qd0", "qd1")}

        if stage == "d":
            finish_zero_out()
            return nc

        with ExitStack() as stE:
            EE = stE.enter_context
            g3rb = EE(nc.sbuf_tensor("g3rb", [128, HID], BF16))
            hr = EE(nc.sbuf_tensor("hr", [128, 2, HID], BF16))
            u3b = EE(nc.sbuf_tensor("u3b", [128, HID], BF16))
            uf32 = EE(nc.sbuf_tensor("uf32", [128, 2048], F32))
            xq3b = EE(nc.sbuf_tensor("xq3b", [128, 2, HID], BF16))
            dmpE = EE(nc.sbuf_tensor("dmpE", [128, HID], BF16))
            q3Tst = EE(nc.sbuf_tensor("q3Tst", [128, 2, HID], BF16))
            est = EE(nc.sbuf_tensor("est", [128, NT, 8], F32))
            rcpE = EE(nc.sbuf_tensor("rcpE", [128, NT, 2], F32))

            ldE = {}
            pB4_E0 = cnt["pB4"]
            pE_E0 = cnt["pE"]
            pSq_E0 = cnt["pQ"]
            dqE_base2 = {0: cnt["dq0"], 1: cnt["dq1"]}

            @blk.gpsimd
            def _(g):
                for n_, v_ in qD_tot.items():
                    wt(g, n_, v_)
                ldE["g3"] = inct(g.dma_start(out=g3rb[:], in_=g3r_in), "gg12")
                for tt in range(NT):
                    if tt >= 2:
                        wt(g, "pE2", 2 * (tt - 1))
                        wt(g, "pQ3", tt - 1)
                    ldE[f"h{tt}"] = inct(g.dma_start(out=hr[:, tt % 2, :],
                                                     in_=spl_h.ap()[tt]), f"ghr{tt % 2}")

            @blk.vector
            def _(v):
                wtt(v, ldE["g3"])
                for tt in range(NT):
                    if tt >= 1:
                        for b in range(4):
                            bidx = 4 * (tt - 1) + b
                            wt(v, "pB4", pB4_E0 + bidx + 1)
                            if b == 0 and tt >= 3:
                                wt(v, f"qe{(tt - 1) % 2}", 16 * ((tt - 3) // 2 + 1))
                                if "dbg_q3" in dbg_t:
                                    wt(v, f"dq{(tt - 1) % 2}",
                                       dqE_base2[(tt - 1) % 2] + 16 * ((tt - 3) // 2 + 1))
                            ins = v.tensor_copy(
                                q3Tst[:, (tt - 1) % 2, b * 2048:(b + 1) * 2048],
                                P1b[:, 0:2048])
                            inc(v.drain(), "pE", 1)
                    wtt(v, ldE[f"h{tt}"])
                    v.tensor_tensor(u3b[:], hr[:, tt % 2, :], g3rb[:], ALU.mult)
                    ins = v.reduce_max(est[:, tt, 1:2], u3b[:], axis=AX.X,
                                       apply_absolute_value=True)
                    inc(ins, "pE2", 1)
                    wt(v, "pE2", 2 * tt + 1)
                    ins = v.tensor_scalar(est[:, tt, 2:3], est[:, tt, 0:1], 1.0 / HID,
                                          1e-8, ALU.mult, ALU.add)
                    inc(v.drain(), "v", 1)
                    wt(v, "pQ", pSq_E0 + tt + 1)
                    if tt >= 2:
                        wt(v, "pB4", pB4_E0 + 4 * (tt - 1))
                    v.reciprocal(rcpE[:, tt, 0:1], est[:, tt, 3:4])
                    v.drain()
                    v.tensor_scalar(est[:, tt, 5:6], est[:, tt, 1:2],
                                    rcpE[:, tt, 0:1], 1e-4, ALU.mult, ALU.max)
                    v.drain()
                    v.reciprocal(rcpE[:, tt, 1:2], est[:, tt, 5:6])
                    v.drain()
                    v.scalar_tensor_tensor(est[:, tt, 7:8], rcpE[:, tt, 1:2], 127.0,
                                           rcpE[:, tt, 0:1], ALU.mult, ALU.mult)
                    v.scalar_tensor_tensor(rs123[:, 2, tt:tt + 1], est[:, tt, 5:6],
                                           1.0 / 127.0, sland[:, 8:9], ALU.mult, ALU.mult)
                    v.drain()
                    ins = None
                    for sl in range(4):
                        v.scalar_tensor_tensor(uf32[:],
                                               hr[:, tt % 2, sl * 2048:(sl + 1) * 2048],
                                               est[:, tt, 7:8],
                                               g3rb[:, sl * 2048:(sl + 1) * 2048],
                                               ALU.mult, ALU.mult)
                        v.drain()
                        ins = v.tensor_scalar(xq3b[:, tt % 2, sl * 2048:(sl + 1) * 2048],
                                              uf32[:], MAGIC, -MAGIC, ALU.add, ALU.add)
                    inc(v.drain(), "pQ3", 1)
                for b in range(4):
                    bidx = 4 * (NT - 1) + b
                    wt(v, "pB4", pB4_E0 + bidx + 1)
                    ins = v.tensor_copy(q3Tst[:, (NT - 1) % 2, b * 2048:(b + 1) * 2048],
                                        P1b[:, 0:2048])
                    inc(v.drain(), "pE", 1)

            @blk.scalar
            def _(s):
                for tt in range(NT):
                    wtt(s, ldE[f"h{tt}"])
                    s.activation(dmpE[:], hr[:, tt % 2, :],
                                 AF.Square, accum_out=est[:, tt, 0:1])
                    inc(s.drain(), "pE2", 1)
                    wt(s, "v", cnt["v"] - (NT - 1 - tt))
                    ins = s.activation(est[:, tt, 3:4], est[:, tt, 2:3], AF.Sqrt)
                    inc(s.drain(), "pQ", 1)

            @blk.tensor
            def _(t):
                wt(t, "gsu", cnt["gsu"])
                for tt in range(NT):
                    wt(t, "pQ3", tt + 1)
                    for b in range(4):
                        bidx = 4 * tt + b
                        wt(t, "pE", pE_E0 + bidx)
                        ins = None
                        for d in range(16):
                            ins = t.transpose(
                                P1b[:, d * 128:(d + 1) * 128],
                                xq3b[:, tt % 2, (b * 16 + d) * 128:(b * 16 + d + 1) * 128],
                                ident[:])
                        inc(ins, "pB4", 1)

            @blk.sync
            def _(sy):
                for tt in range(NT):
                    wt(sy, "pE", pE_E0 + 4 * tt + 4)
                    inc(sy.dma_start(out=spl_q3T.ap()[tt], in_=q3Tst[:, tt % 2, :]),
                        f"qe{tt % 2}", 16)
                    if "dbg_q3" in dbg_t:
                        inc(sy.dma_start(out=dbg_t["dbg_q3"][tt], in_=q3Tst[:, tt % 2, :]),
                            f"dq{tt % 2}", 16)
                if "dbg_est" in dbg_t:
                    inc(sy.dma_start(out=dbg_t["dbg_est"], in_=est[:]), "dq", 16)
            qE_tot = {n: cnt[n] for n in ("qe0", "qe1")}

        with ExitStack() as stF:
            EF = stF.enter_context
            x3T = EF(nc.sbuf_tensor("x3T", [128, 4, HID], BF16))
            x1r = EF(nc.sbuf_tensor("x1r", [128, 4, D], BF16))
            w3f = EF(nc.sbuf_tensor("w3f", [128, 2, 4, 512], F32))
            w3q = EF(nc.sbuf_tensor("w3q", [128, 2, 4, 512], BF16))
            r2 = EF(nc.sbuf_tensor("r2", [128, 4, D], F32))
            r2p = EF(nc.sbuf_tensor("r2p", [128, 4, 4], F32))
            fst = EF(nc.sbuf_tensor("fst", [128, 4, 8], F32))
            rcpF = EF(nc.sbuf_tensor("rcpF", [128, 4, 2], F32))
            dmpF = EF(nc.sbuf_tensor("dmpF", [128, D], F32))
            outst = EF(nc.sbuf_tensor("outst", [128, 2, D], F32))

            gxf_tot = {}
            for half in range(2):
                tiles = [half * 4 + i for i in range(4)]
                ldF = {}
                pF2_h0 = cnt["pF2"]
                pB6_h0 = cnt["pB6"]
                pF_h0 = cnt["pF"]

                @blk.gpsimd
                def _(g):
                    for n_, v_ in qE_tot.items():
                        wt(g, n_, v_)
                    if half == 1:
                        wt(g, "pF", pF_h0)
                    for i, tt in enumerate(tiles):
                        ldF[f"x3{i}"] = inc(g.dma_start(out=x3T[:, i, :],
                                                        in_=spl_q3T.ap()[tt]), "gxf", 16)
                        ldF[f"x1{i}"] = inc(g.dma_start(out=x1r[:, i, :],
                                                        in_=spl_x1.ap()[tt]), "gxf", 16)
                    ngr = 0
                    for osl in range(4):
                        for hq in range(16):
                            if ngr >= 2:
                                wt(g, "pF2", pF2_h0 + ngr - 1)
                            ldF[f"w{ngr}"] = inct(g.dma_start(
                                out=w3f[:, ngr % 2, :, :],
                                in_=w3T_in[hq * 512:(hq + 1) * 512,
                                           osl * 512:(osl + 1) * 512]
                                    .rearrange("(c p) n -> p c n", p=128)), f"gw3{ngr % 2}")
                            ngr += 1
                    gxf_tot[half] = cnt["gxf"]

                @blk.vector
                def _(v):
                    def evacF(osl):
                        wt(v, "pB6", pB6_h0 + 16 * (osl + 1))
                        wt(v, "gxf", gxf_tot[half])
                        for i in range(4):
                            ins = v.scalar_tensor_tensor(
                                r2[:, i, osl * 512:(osl + 1) * 512],
                                f3_slabs[i], rs123[:, 2, tiles[i]:tiles[i] + 1],
                                x1r[:, i, osl * 512:(osl + 1) * 512],
                                ALU.mult, ALU.add,
                                accum_out=r2p[:, i, osl:osl + 1])
                            inc(ins, "pF", 1)

                    ngr = 0
                    for osl in range(4):
                        if osl >= 1:
                            evacF(osl - 1)
                        for hq in range(16):
                            wtt(v, ldF[f"w{ngr}"])
                            if ngr >= 2:
                                wt(v, "pB6", pB6_h0 + ngr - 1)
                            v.tensor_scalar(w3f[:, ngr % 2, :, :], w3f[:, ngr % 2, :, :],
                                            sland[:, 5:6], MAGIC, ALU.mult, ALU.add)
                            v.tensor_scalar(w3f[:, ngr % 2, :, :], w3f[:, ngr % 2, :, :],
                                            -MAGIC, 1.0, ALU.add, ALU.min)
                            ins = v.tensor_scalar(w3q[:, ngr % 2, :, :],
                                                  w3f[:, ngr % 2, :, :],
                                                  -1.0, None, ALU.max)
                            inc(v.drain(), "pF2", 1)
                            ngr += 1
                    evacF(3)

                @blk.tensor
                def _(t):
                    wt(t, "gxf", gxf_tot[half])
                    ngr = 0
                    for osl in range(4):
                        if osl >= 1 or half == 1:
                            wt(t, "pF", pF_h0 + 4 * osl if osl >= 1 else pF_h0)
                        for hq in range(16):
                            wt(t, "pF2", pF2_h0 + ngr + 1)
                            ins = None
                            for c4 in range(4):
                                hc = 4 * hq + c4
                                for i in range(4):
                                    ins = t.matmul(f3_slabs[i],
                                                   x3T[:, i, hc * 128:(hc + 1) * 128],
                                                   w3q[:, ngr % 2, c4, :],
                                                   start=(hc == 0), stop=(hc == NHC - 1))
                            inc(ins, "pB6", 1)
                            ngr += 1

                @blk.vector
                def _(v):
                    wt(v, "pF", pF_h0 + 16)
                    v.drain()
                    ins = None
                    for i in range(4):
                        v.reduce_sum(fst[:, i, 0:1], r2p[:, i, :], axis=AX.X)
                    v.drain()
                    for i in range(4):
                        ins = v.tensor_scalar(fst[:, i, 1:2], fst[:, i, 0:1],
                                              1.0 / D, None, ALU.mult)
                    inc(v.drain(), "v", 1)
                vF_a = cnt["v"]

                @blk.scalar
                def _(s):
                    wt(s, "v", vF_a)
                    ins = None
                    for i in range(4):
                        s.activation(fst[:, i, 3:4], fst[:, i, 1:2], AF.Square)
                        ins = s.activation(dmpF[:], r2[:, i, :], AF.Square,
                                           accum_out=fst[:, i, 2:3])
                    inc(s.drain(), "s", 1)
                sF_b = cnt["s"]

                @blk.vector
                def _(v):
                    wt(v, "s", sF_b)
                    ins = None
                    for i in range(4):
                        v.scalar_tensor_tensor(fst[:, i, 4:5], fst[:, i, 2:3], 1.0 / D,
                                               fst[:, i, 3:4], ALU.mult, ALU.subtract)
                        v.drain()
                        ins = v.tensor_scalar(fst[:, i, 4:5], fst[:, i, 4:5],
                                              1e-5, None, ALU.add)
                    inc(v.drain(), "v", 1)
                vF_c = cnt["v"]

                @blk.scalar
                def _(s):
                    wt(s, "v", vF_c)
                    ins = None
                    for i in range(4):
                        ins = s.activation(fst[:, i, 5:6], fst[:, i, 4:5], AF.Sqrt)
                    inc(s.drain(), "s", 1)
                sF_d = cnt["s"]

                vout_snap = {}

                @blk.vector
                def _(v):
                    wt(v, "s", sF_d)
                    for i in range(4):
                        if i >= 2:
                            wt(v, f"qo{i % 2}", 16 * (2 * half + i // 2))
                        v.reciprocal(rcpF[:, i, 0:1], fst[:, i, 5:6])
                        v.drain()
                        ins = v.tensor_scalar(outst[:, i % 2, :], r2[:, i, :],
                                              fst[:, i, 1:2], rcpF[:, i, 0:1],
                                              ALU.subtract, ALU.mult)
                        vout_snap[i] = inc(v.drain(), "v", 1)

                @blk.sync
                def _(sy):
                    for i, tt in enumerate(tiles):
                        wt(sy, "v", vout_snap[i])
                        inc(sy.dma_start(out=out_t[tt * 128:(tt + 1) * 128, :],
                                         in_=outst[:, i % 2, :]), f"qo{i % 2}", 16)

        @blk.scalar
        def _(s):
            for n_ in cnt:
                if (n_[0] == "q" or n_[:2] == "dq") and cnt[n_] > 0:
                    wt(s, n_, cnt[n_])

    return nc


def _host_inputs(inputs):
    x = np.asarray(inputs["L0_attn_x"])[0]
    w_qkv = np.ascontiguousarray(np.asarray(inputs["L0_attn_w_qkv"]), dtype=np.float32)
    w_out = np.ascontiguousarray(np.asarray(inputs["L0_attn_w_out"]), dtype=np.float32)
    decay = np.ascontiguousarray(np.asarray(inputs["L0_attn_decay_param"]), dtype=np.float32)
    w1 = np.asarray(inputs["L0_ffn_w1"])
    w2 = np.asarray(inputs["L0_ffn_w2"])
    w3 = np.asarray(inputs["L0_ffn_w3"])
    g1 = np.asarray(inputs["L0_ffn_g1"])
    g2 = np.asarray(inputs["L0_ffn_g2"])
    g3 = np.asarray(inputs["L0_ffn_g3"])

    xT = np.ascontiguousarray(x.T)
    w1T = np.ascontiguousarray(w1.T)
    w2T = np.ascontiguousarray(w2.T)
    w3T = np.ascontiguousarray(w3.T)

    g1r = np.ascontiguousarray(np.broadcast_to(g1[None, :], (128, D)), dtype=np.float32)
    g2r = np.ascontiguousarray(np.broadcast_to(g2[None, :], (128, D)), dtype=np.float32)
    g3r = np.ascontiguousarray(np.broadcast_to(g3[None, :], (128, HID)), dtype=np.float32)
    drep = np.ascontiguousarray(np.broadcast_to(decay[None, :], (128, H)), dtype=np.float32)
    d16 = np.ascontiguousarray(decay.reshape(16, 1), dtype=np.float32)

    jj, tt_ = np.meshgrid(np.arange(128), np.arange(128), indexing="ij")
    mmat = np.where(tt_ >= jj, (tt_ - jj).astype(np.float32), np.float32(1e30))
    rows127 = np.ascontiguousarray(
        np.broadcast_to((127 - np.arange(128, dtype=np.float32))[:, None], (128, H)))
    iota16 = np.ascontiguousarray(
        np.broadcast_to((np.arange(128, dtype=np.float32) + 1.0)[None, :], (16, 128)))
    cexpo = np.ascontiguousarray(
        np.broadcast_to((128.0 * (7 - np.arange(8, dtype=np.float32)))[None, :], (16, 8)))
    identb = np.eye(128, dtype=ml_dtypes.bfloat16)

    in_maps = []
    for me in range(NCORE):
        eexpo = np.full((16, 8), 1e30, dtype=np.float32)
        for j in range(me):
            eexpo[:, j] = 1024.0 * (me - 1 - j)
        m = dict(
            x=np.ascontiguousarray(x[me * TC:(me + 1) * TC, :], dtype=np.float32),
            xT=np.ascontiguousarray(xT[:, me * TC:(me + 1) * TC], dtype=np.float32),
            w_qkv=w_qkv, w_out=w_out,
            w1T=w1T, w2T=w2T, w3T=w3T,
            ws1=np.ascontiguousarray(w1T[me * 256:(me + 1) * 256, :], dtype=np.float32),
            ws2=np.ascontiguousarray(w2T[me * 256:(me + 1) * 256, :], dtype=np.float32),
            ws3=np.ascontiguousarray(w3T[me * 1024:(me + 1) * 1024, :], dtype=np.float32),
            g1r=g1r, g2r=g2r, g3r=g3r, drep=drep, d16=d16,
            mmat=mmat, rows127=rows127, iota16=iota16,
            cexpo=cexpo, eexpo=eexpo, identb=identb,
            eye16=np.eye(16, dtype=np.float32),
            onesf=np.ones((128, 128), dtype=np.float32),
        )
        in_maps.append(m)
    return in_maps


def kernel(**inputs):
    in_maps = _host_inputs(inputs)
    nc = _build(_STAGE, list(_DBG))
    res = run_bass_kernel_spmd(nc, in_maps, core_ids=list(range(NCORE)))
    out = np.concatenate([np.asarray(res.results[c]["out"]) for c in range(NCORE)], axis=0)
    if _DBG:
        kernel._dbg = res.results
    return out.reshape(1, NCORE * TC, D).astype(np.float32)
